# version 18
# baseline (speedup 1.0000x reference)
import numpy as np
import concourse.bass as bass
import concourse.mybir as mybir
from concourse.bass_utils import run_bass_kernel_spmd
from contextlib import ExitStack

F32 = mybir.dt.float32
BF16 = mybir.dt.bfloat16
AF = mybir.ActivationFunctionType
ALU = mybir.AluOpType

D = 1024
NCH = 8
NCORES = 8
SEQ_PER_CORE = 2
SEQLEN = 2048
NT = SEQLEN // 128
ROWS = SEQ_PER_CORE * SEQLEN
EPS_LN = 1e-5
EPS_ATTN = 1e-6

KVB = [0, 129, 258, 512, 641, 770, 1024, 1153]



class _FakeInst:
    def then_inc(self, *a, **k):
        return self


class _FakeEng:
    def __init__(self):
        self.calls = []

    def __getattr__(self, name):
        def rec(*args, **kwargs):
            shapes = []
            for a in list(args) + list(kwargs.values()):
                sh = getattr(a, 'shape', None)
                if sh is not None:
                    shapes.append(tuple(sh))
            self.calls.append((name, shapes))
            return _FakeInst()
        return rec


def _op_cost(op):
    f = _FakeEng()
    try:
        op["fn"](f)
        name, shapes = f.calls[0]
    except Exception:
        return 500.0
    eng = op["engine"]

    def nel(sh):
        n = 1
        for d in sh[1:]:
            n *= d
        return n
    if name == "matmul":
        N = nel(shapes[0])
        P = nel(shapes[1])
        return max(N / 2.4, P / 1.2) + 60
    if name == "transpose":
        return 275.0
    if name == "dma_start":
        n = 1
        for d in shapes[0]:
            n *= d
        return 1700.0 + n * 4 / 55.0
    if name == "bn_stats":
        return 680.0
    if name == "bn_aggr":
        return 210.0
    if name == "reciprocal":
        return 860.0
    if name in ("memset", "nop"):
        return 120.0
    n = nel(shapes[0]) if shapes else 128
    if name == "activation":
        return (200 + n) / 1.2
    if name == "copy":
        return (172 + n) / 1.2
    if name == "tensor_copy":
        return 2.9 * n if eng == "gpsimd" else (120 + n / 2) / 0.96 + 60
    if name in ("tensor_tensor", "tensor_add", "tensor_sub"):
        return 2.2 * n if eng == "gpsimd" else (151 + n) / 0.96 + 60
    if name == "scalar_tensor_tensor":
        return 2.2 * n if eng == "gpsimd" else (151 + n) / 0.96 + 600
    if name in ("tensor_scalar", "tensor_scalar_add"):
        return 1.5 * n if eng == "gpsimd" else (58 + n / 2) / 0.96 + 60
    return 500.0


class Plan:

    DMA_ROT = 3

    def __init__(self):
        self.ops = []
        self.keys = {}
        self.dma_hist = {}

    def op(self, engine, fn, reads=(), writes=(), dma=False, force=(), cost=500,
           cluster=None):
        oid = len(self.ops)
        d = set()
        for k in reads:
            st = self.keys.get(k)
            if st:
                d.update(st["w"])
        for k in writes:
            st = self.keys.get(k)
            if st:
                d.update(st["w"])
                d.update(st["r"])
        for k in reads:
            self.keys.setdefault(k, {"w": [], "r": []})["r"].append(oid)
        for k in writes:
            st = self.keys.setdefault(k, {"w": [], "r": []})
            st["w"] = [oid]
            st["r"] = []
        d.discard(oid)
        d.update(force)
        sem = None
        if dma:
            hist = self.dma_hist.setdefault(engine, [])
            if len(hist) >= self.DMA_ROT:
                d.add(hist[len(hist) - self.DMA_ROT])
            sem = f"dma_{engine}{len(hist) % self.DMA_ROT}"
            hist.append(oid)
        self.ops.append({"engine": engine, "fn": fn, "deps": sorted(d),
                         "dma": dma, "inc": False, "tok": None, "sem": sem,
                         "force": set(force), "cost": cost,
                         "cluster": cluster})
        return oid

    def schedule(self):
        n = len(self.ops)
        cost = [_op_cost(o) for o in self.ops]
        succ = [[] for _ in range(n)]
        npred = [0] * n
        for i, o in enumerate(self.ops):
            for d in o["deps"]:
                succ[d].append(i)
                npred[i] += 1
        prio = [0.0] * n
        for i in range(n - 1, -1, -1):
            m = 0.0
            for j in succ[i]:
                if prio[j] > m:
                    m = prio[j]
            prio[i] = m + cost[i]
        import os
        W = int(os.environ.get("KSCHED_W", "2500"))
        LO = int(os.environ.get("KSCHED_LO", "0"))
        HI = int(os.environ.get("KSCHED_HI", str(n)))
        PIN = set(os.environ.get("KSCHED_PIN", "").split(",")) - {""}
        if PIN:
            last_pin = {}
            for i, o in enumerate(self.ops):
                tag = None
                if o["engine"] in PIN:
                    tag = o["engine"]
                if "dma" in PIN and o["dma"]:
                    tag = "dma_" + o["engine"]
                if tag is not None:
                    if tag in last_pin:
                        d0 = last_pin[tag]
                        if d0 not in o["deps"]:
                            o["deps"] = sorted(o["deps"] + [d0])
                    last_pin[tag] = i
        clusters = {}
        for i, o in enumerate(self.ops):
            if o["cluster"] is not None:
                clusters.setdefault(o["cluster"], []).append(i)
        cl_members = {}
        for cid, mem in clusters.items():
            for i in mem:
                cl_members[i] = mem
        eng_free = {}
        end = [0.0] * n
        ready = set(i for i in range(n) if npred[i] == 0)
        order = []
        dep_ready = [0.0] * n
        nsched = 0
        sched_mark = [False] * n
        frontier = 0
        while ready:
            while frontier < n and sched_mark[frontier]:
                frontier += 1
            best = None
            best_key = None
            for i in ready:
                if i != frontier and not (LO <= i < HI and LO <= frontier < HI
                                          and i <= frontier + W):
                    continue
                grp = cl_members.get(i)
                if grp is not None:
                    if i != grp[0]:
                        continue
                    gs = set(grp)
                    if not all(all((d in gs) or sched_mark[d]
                                   for d in self.ops[m]["deps"]) for m in grp):
                        continue
                e = self.ops[i]["engine"]
                st = max(eng_free.get(e, 0.0), dep_ready[i])
                key = (st, -prio[i])
                if best_key is None or key < best_key:
                    best_key = key
                    best = i
            if best is None:
                best = frontier
            i = best
            group = cl_members.get(i, [i])
            for m in group:
                if sched_mark[m]:
                    continue
                ready.discard(m)
                sched_mark[m] = True
                om = self.ops[m]
                em = om["engine"]
                stm = max(eng_free.get(em, 0.0), dep_ready[m])
                end[m] = stm + cost[m]
                eng_free[em] = stm + (150.0 if om["dma"] else cost[m])
                order.append(m)
                for jj in succ[m]:
                    npred[jj] -= 1
                    tt = end[m] + (110.0 if em != self.ops[jj]["engine"] else 0.0)
                    if tt > dep_ready[jj]:
                        dep_ready[jj] = tt
                    if npred[jj] == 0 and not sched_mark[jj]:
                        ready.add(jj)
            continue
        assert len(order) == n
        remap = {old: new for new, old in enumerate(order)}
        new_ops = []
        for new, old in enumerate(order):
            o = self.ops[old]
            o["deps"] = sorted(remap[d] for d in o["deps"])
            o["force"] = set(remap[d] for d in o["force"])
            new_ops.append(o)
        self.ops = new_ops

    def resolve(self):
        n = len(self.ops)
        cnt0 = {}
        tok0 = [None] * n
        for i, o in enumerate(self.ops):
            sem = o["sem"] if o["dma"] else ("eng_" + o["engine"])
            amt = 16 if o["dma"] else 1
            cnt0[sem] = cnt0.get(sem, 0) + amt
            tok0[i] = (sem, cnt0[sem], amt)
        ic = {}
        cc = [None] * n
        waitlists = [None] * n
        waited = set()
        for i, o in enumerate(self.ops):
            e = o["engine"]
            myic = ic.setdefault(e, {})
            need = []
            for d in sorted(o["deps"], key=lambda d: -tok0[d][1]):
                dop = self.ops[d]
                if (e == "tensor" and dop["engine"] == "tensor"
                        and not dop["dma"] and not o["dma"]
                        and d not in o["force"]):
                    continue
                s, v, _ = tok0[d]
                if myic.get(s, 0) >= v:
                    continue
                need.append(d)
                waited.add(d)
                for s2, v2 in cc[d].items():
                    if myic.get(s2, 0) < v2:
                        myic[s2] = v2
                if myic.get(s, 0) < v:
                    myic[s] = v
            waitlists[i] = need
            mycc = dict(myic)
            s, v, _ = tok0[i]
            mycc[s] = max(mycc.get(s, 0), v)
            cc[i] = mycc
        counts = {}
        for i, o in enumerate(self.ops):
            o["inc"] = o["dma"] or (i in waited)
            if not o["inc"]:
                o["tok"] = None
                continue
            sem = o["sem"] if o["dma"] else ("eng_" + o["engine"])
            amt = 16 if o["dma"] else 1
            counts[sem] = counts.get(sem, 0) + amt
            o["tok"] = (sem, counts[sem], amt)
        self._waitlists = waitlists
        self.final_counts = counts

    def emit_engine(self, name, eng, sems):
        for i, o in enumerate(self.ops):
            if o["engine"] != name:
                continue
            for dep in self._waitlists[i]:
                sem, val, _ = self.ops[dep]["tok"]
                eng.wait_ge(sems[sem], val)
            inst = o["fn"](eng)
            if o["inc"]:
                sem, _, amt = o["tok"]
                inst.then_inc(sems[sem], amt)


def build_nc():
    nc = bass.Bass("TRN2", target_bir_lowering=False)

    x_d = nc.dram_tensor("x", [ROWS, D], F32, kind="ExternalInput").ap()
    wqblk_d = nc.dram_tensor("wqblk", [NCH, 128, 128], F32, kind="ExternalInput").ap()
    wkvblk_d = nc.dram_tensor("wkvblk", [NCH, 128, 256], F32, kind="ExternalInput").ap()
    wo_d = nc.dram_tensor("wo", [D, D], F32, kind="ExternalInput").ap()
    wt_d = nc.dram_tensor("wt", [D, D], F32, kind="ExternalInput").ap()
    a1_d = nc.dram_tensor("a1", [D, D], F32, kind="ExternalInput").ap()
    a2_d = nc.dram_tensor("a2", [D, D], F32, kind="ExternalInput").ap()
    bias_d = nc.dram_tensor("bias", [1, D], F32, kind="ExternalInput").ap()
    sel_d = nc.dram_tensor("sel", [16, NCH * 128], F32, kind="ExternalInput").ap()
    eye_d = nc.dram_tensor("eye", [128, 128], F32, kind="ExternalInput").ap()
    y_d = nc.dram_tensor("y", [ROWS, D], F32, kind="ExternalOutput").ap()

    A = nc.alloc_sbuf_tensor
    wq_b = A("wq_b", [128, NCH * 128], BF16)
    wkv_b = A("wkv_b", [128, NCH * 256], BF16)
    wo_b = A("wo_b", [128, NCH * 1024], BF16)
    p_b = A("p_b", [128, NCH * 1024], BF16)
    eye_b = A("eye_b", [128, 128], BF16)
    sel_b = A("sel_b", [16, NCH * 128], BF16)
    bias_b = A("bias_b", [1, D], BF16)
    onesr_b = A("onesr_b", [1, 128], BF16)
    one_s = A("one_s", [128, 1], F32)
    eps_s = A("eps_s", [128, 1], F32)
    eps4_s = A("eps4_s", [128, 1], F32)
    xn_cache = A("xn_cache", [128, NCH * SEQLEN], BF16)
    kvblk_b = A("kvblk_b", [128, NCH * 128], BF16)
    zblk_b = A("zblk_b", [128, NCH * 16], BF16)
    kf_b = [A(f"kf_b{i}", [128, D], BF16) for i in range(2)]
    vext_b = [A(f"vext_b{i}", [128, NCH * 129], BF16) for i in range(2)]
    xn_tm = [A(f"xn_tm{i}", [128, D], BF16) for i in range(2)]
    relu_b = [A(f"relu_b{i}", [128, D], BF16) for i in range(2)]
    exp_b = [A(f"exp_b{i}", [128, D], BF16) for i in range(2)]
    x_tm = [A(f"x_tm{i}", [128, D], F32) for i in range(4)]
    out_b = [A(f"out_b{i}", [128, D], F32) for i in range(2)]
    den_f = [A(f"den_f{i}", [16, 128], F32) for i in range(2)]
    rden_f = [A(f"rden_f{i}", [16, 128], F32) for i in range(2)]
    rden_b = [A(f"rden_b{i}", [16, 128], BF16) for i in range(2)]

    def statbufs(name, n=2):
        return [dict(
            bn=A(f"{name}_bn{i}", [128, 12], F32),
            ag=A(f"{name}_ag{i}", [128, 2], F32),
            ve=A(f"{name}_ve{i}", [128, 1], F32),
            l=A(f"{name}_l{i}", [128, 1], F32),
            r=A(f"{name}_r{i}", [128, 1], F32),
            mb=A(f"{name}_mb{i}", [128, 1], F32),
        ) for i in range(n)]
    st1 = statbufs("ln1")
    st2 = statbufs("ln2")
    st3 = statbufs("ln3")
    st4 = statbufs("ln4")

    big_b = A("big_b", [128, 32768], BF16)
    wt_b = big_b[:, 0:8192]
    a1_b = big_b[:, 8192:16384]
    a2_b = big_b[:, 16384:24576]
    qt_b = big_b[:, 24576:32768]
    bigf = big_b.bitcast(F32)
    bslc = lambda i: big_b[:, 1024 * i:1024 * (i + 1)]
    fslc = lambda i: bigf[:, 1024 * i:1024 * (i + 1)]
    qf_b = [bslc(0), bslc(1)]
    qf2_b = [bslc(2), bslc(3)]
    rdenbc_b = [bslc(4), bslc(5)]
    ctx_b = [bslc(6), bslc(7)]
    xn2tm_b = [bslc(8), bslc(9)]
    xn2f_b = [bslc(10), bslc(11)]
    lnS = [fslc(6), fslc(7)]
    lnS4 = [fslc(8), fslc(9)]
    x1_b = [fslc(10), fslc(11)]
    t_b = [fslc(12), fslc(13)]
    s_b = [fslc(14), fslc(15)]
    stage = [A("stage0", [128, 2048], F32)[:], A("stage1", [128, 2048], F32)[:]]
    rem = nc.sbuf_bytes_remaining
    assert rem >= 0, f"SBUF over budget by {-rem} bytes"

    PS_A = nc.alloc_psum_tensor("PS_A", [128, 1536], F32)
    PS_B = nc.alloc_psum_tensor("PS_B", [128, 1024], F32)
    PS_C = nc.alloc_psum_tensor("PS_C", [128, 1024], F32)
    PS_T = nc.alloc_psum_tensor("PS_T", [128, 1024], BF16)

    p = Plan()
    _clc = [0]

    def new_cluster():
        _clc[0] += 1
        return _clc[0]

    def dma(engine, out_ap, in_ap, reads=(), writes=()):
        return p.op(engine, lambda e, o=out_ap, i=in_ap: e.dma_start(out=o, in_=i),
                    reads=reads, writes=writes, dma=True)

    dma("sync", stage[0][:, 0:1024].rearrange("p (a b) -> p a b", a=NCH),
        wqblk_d.rearrange("a p b -> p a b"), writes=[("stg", 0)])
    p.op("gpsimd", lambda e: e.tensor_copy(wq_b[:], stage[0][:, 0:1024]),
         reads=[("stg", 0)], writes=[("wq",)])
    dma("sync", stage[0][:, 1024:2048].rearrange("p (a b) -> p a b", a=4),
        wkvblk_d.rearrange("a p b -> p a b")[:, 0:4, :], writes=[("stg", 1)])
    p.op("gpsimd", lambda e: e.tensor_copy(
        wkv_b[:, 0:1024].rearrange("p (a b) -> p a b", a=4),
        stage[0][:, 1024:2048].rearrange("p (a b) -> p a b", a=4)),
         reads=[("stg", 1)], writes=[("wkv", 0)])
    dma("sync", stage[1][:, 0:1024].rearrange("p (a b) -> p a b", a=4),
        wkvblk_d.rearrange("a p b -> p a b")[:, 4:8, :], writes=[("stg", 2)])
    p.op("gpsimd", lambda e: e.tensor_copy(
        wkv_b[:, 1024:2048].rearrange("p (a b) -> p a b", a=4),
        stage[1][:, 0:1024].rearrange("p (a b) -> p a b", a=4)),
         reads=[("stg", 2)], writes=[("wkv", 1)])
    dma("sync", stage[1][:, 1024:1152], eye_d, writes=[("stg", 3)])
    p.op("gpsimd", lambda e: e.tensor_copy(eye_b[:], stage[1][:, 1024:1152]),
         reads=[("stg", 3)], writes=[("eye",)])
    dma("sync", stage[1][0:16, 0:1024], sel_d, reads=[("wkv", 1)],
        writes=[("stg", 4)])
    p.op("gpsimd", lambda e: e.tensor_copy(sel_b[:], stage[1][0:16, 0:1024]),
         reads=[("stg", 4)], writes=[("sel",)])
    dma("sync", stage[0][0:1, 0:1024], bias_d, reads=[("wq",)], writes=[("stg", 5)])
    p.op("gpsimd", lambda e: e.tensor_copy(bias_b[:], stage[0][0:1, 0:1024]),
         reads=[("stg", 5)], writes=[("biasb",)])
    p.op("vector", lambda e: e.memset(one_s[:], 1.0), writes=[("one",)])
    p.op("vector", lambda e: e.memset(eps_s[:], float(EPS_LN)), writes=[("eps",)])
    p.op("vector", lambda e: e.memset(eps4_s[:], float(EPS_LN / 4.0)), writes=[("eps4",)])
    p.op("vector", lambda e: e.memset(onesr_b[:], 1.0), writes=[("onesr",)])
    for i in range(2):
        vv = vext_b[i].rearrange("p (c n) -> p c n", n=129)
        p.op("vector", lambda e, v3=vv: e.memset(v3[:, :, 128:129], 1.0),
             writes=[("vext1", i)])

    def big_load(queue, src, dst, key, sslot, skeyb, first_reads):
        for h in range(8):
            half = h % 2
            sl = sslot[:, half * 1024:(half + 1) * 1024]
            skey = (skeyb, half)
            dma(queue, sl.rearrange("p (a b) -> p a b", a=1),
                src.rearrange("(c p) n -> p c n", p=128)[:, h:h + 1, :],
                reads=first_reads if h < 2 else (), writes=[skey])
            p.op("gpsimd",
                 lambda e, d2=dst, s2=sl, hh=h: e.tensor_copy(
                     d2[:, hh * 1024:(hh + 1) * 1024], s2),
                 reads=[skey], writes=[(key, h // 2)])
    big_load("scalar", wt_d, wt_b, "wt", stage[0][:, 0:2048], "stageA",
             [("wq",), ("wkv", 0), ("biasb",)])
    big_load("scalar", wo_d, wo_b, "wo", stage[0][:, 0:2048], "stageA", [])
    big_load("gpsimd", a1_d, a1_b, "a1", stage[1][:, 0:2048], "stageB",
             [("wkv", 1), ("eye",), ("sel",)])
    big_load("gpsimd", a2_d, a2_b, "a2", stage[1][:, 0:2048], "stageB", [])
    WOK = [("wo", h) for h in range(4)]
    WTK = [("wt", h) for h in range(4)]
    A1K = [("a1", h) for h in range(4)]
    A2K = [("a2", h) for h in range(4)]

    nload = [0]

    def load_x(seq, t):
        i = nload[0] % 4
        nload[0] += 1
        buf, key = x_tm[i], ("x_tm", i)
        row0 = seq * SEQLEN + t * 128
        dma("sync", buf[:], x_d[row0:row0 + 128, :], writes=[key])
        return buf, key

    def ln_stats(sb, skey, src_ap, src_keys):
        p.op("vector", lambda e, a=src_ap: e.bn_stats(sb["bn"][:, 0:6], a[:, 0:512]),
             reads=src_keys, writes=[(skey, "bn0")])
        o1 = p.op("vector", lambda e, a=src_ap: e.bn_stats(sb["bn"][:, 6:12], a[:, 512:1024]),
                  reads=src_keys, writes=[(skey, "bn1")])
        o2 = p.op("vector", lambda e: e.bn_aggr(sb["ag"][:], sb["bn"][:]),
                  reads=[(skey, "bn0"), (skey, "bn1")], writes=[(skey, "ag")],
                  force=[o1])
        sb["_aggr_oid"] = o2

    def ln_rsqrt(sb, skey, eps):
        eb, ek = (eps4_s, ("eps4",)) if eps < EPS_LN else (eps_s, ("eps",))
        p.op("scalar", lambda e, b2=eb: e.activation(
            sb["l"][:], sb["ag"][:, 1:2], AF.Ln, bias=b2[:]),
             reads=[(skey, "ag"), ek], writes=[(skey, "l")])
        p.op("scalar", lambda e: e.activation(sb["r"][:], sb["l"][:], AF.Exp, scale=-0.5),
             reads=[(skey, "l")], writes=[(skey, "r")])

    def ln_mb(sb, skey):
        p.op("vector", lambda e: e.tensor_scalar(
            sb["mb"][:], sb["ag"][:, 0:1], sb["r"][:], -1.0, ALU.mult, ALU.mult),
             reads=[(skey, "ag"), (skey, "r")], writes=[(skey, "mb")])

    def ln_apply_v(sb, skey, out_ap, in_ap, in_keys, out_keys):
        p.op("vector", lambda e, o=out_ap, a=in_ap: e.tensor_scalar(
            o, a, sb["r"][:], sb["mb"][:], ALU.mult, ALU.add),
             reads=list(in_keys) + [(skey, "r"), (skey, "mb")], writes=out_keys)

    def ln_apply_s(sb, skey, out_ap, in_ap, in_keys, out_keys):
        p.op("scalar", lambda e, o=out_ap, a=in_ap: e.activation(
            o, a, AF.Identity, bias=sb["mb"][:], scale=sb["r"][:]),
             reads=list(in_keys) + [(skey, "r"), (skey, "mb")], writes=out_keys)

    xn3 = xn_cache.rearrange("p (c n) -> p c n", c=NCH)

    def xslice(c, t):
        return xn_cache[:, c * SEQLEN + t * 128: c * SEQLEN + (t + 1) * 128]

    def p1_stats(seq, t, xk):
        j = t % 2
        xbuf, xkey = xk
        ln_stats(st1[j], ("s1", j), xbuf[:], [xkey])
        ln_rsqrt(st1[j], ("s1", j), EPS_LN)

    def p1_apply(seq, t, xk):
        j = t % 2
        xbuf, xkey = xk
        ln_mb(st1[j], ("s1", j))
        ln_apply_s(st1[j], ("s1", j), xn_tm[j][:], xbuf[:], [xkey], [("xn_tm", j)])
        cl = new_cluster()
        for c in range(NCH):
            p.op("tensor", lambda e, cc=c: e.transpose(
                PS_T[:, cc * 128:(cc + 1) * 128],
                xn_tm[j][:, cc * 128:(cc + 1) * 128], eye_b[:]),
                 reads=[("xn_tm", j), ("eye",)], writes=[("PT",)], cluster=cl)
        p.op("scalar", lambda e, tt=t: e.copy(
            xn3[:, :, tt * 128:(tt + 1) * 128],
            PS_T[:].rearrange("p (c n) -> p c n", c=NCH)),
             reads=[("PT",)], writes=[("xnc", t)])

    def p1_kv(seq, t):
        j = t % 2
        cl = new_cluster()
        for c in range(NCH):
            ps = PS_B if c < 4 else PS_C
            pk = ("PB",) if c < 4 else ("PC",)
            wk = ("wkv", 0) if c < 4 else ("wkv", 1)
            p.op("tensor", lambda e, cc=c, tt=t, pp=ps: e.matmul(
                pp[:, (cc % 4) * 256:(cc % 4) * 256 + 256], xslice(cc, tt),
                wkv_b[:, cc * 256:(cc + 1) * 256],
                start=True, stop=True, skip_group_check=True),
                 reads=[("xnc", t), wk], writes=[pk], cluster=cl)
        kview_B = bass.AP(PS_B, 0, [[1024, 128], [256, 4], [1, 128]])
        kview_C = bass.AP(PS_C, 0, [[1024, 128], [256, 4], [1, 128]])
        vview_B = bass.AP(PS_B, 128, [[1024, 128], [256, 4], [1, 128]])
        vview_C = bass.AP(PS_C, 128, [[1024, 128], [256, 4], [1, 128]])
        r3 = relu_b[j][:].rearrange("p (c n) -> p c n", n=128)
        e3 = exp_b[j][:].rearrange("p (c n) -> p c n", n=128)
        p.op("scalar", lambda e: e.activation(r3[:, 0:4, :], kview_B, AF.Relu),
             reads=[("PB",)], writes=[("relu", j, 0)])
        p.op("scalar", lambda e: e.activation(r3[:, 4:8, :], kview_C, AF.Relu),
             reads=[("PC",)], writes=[("relu", j, 1)])
        p.op("scalar", lambda e: e.activation(e3[:, 0:4, :], kview_B, AF.Exp),
             reads=[("PB",)], writes=[("exp", j, 0)])
        p.op("scalar", lambda e: e.activation(e3[:, 4:8, :], kview_C, AF.Exp),
             reads=[("PC",)], writes=[("exp", j, 1)])
        v3 = vext_b[j].rearrange("p (c n) -> p c n", n=129)
        p.op("scalar", lambda e: e.copy(v3[:, 0:4, 0:128], vview_B),
             reads=[("PB",)], writes=[("vext", j, 0)])
        p.op("scalar", lambda e: e.copy(v3[:, 4:8, 0:128], vview_C),
             reads=[("PC",)], writes=[("vext", j, 1)])
        p.op("vector", lambda e: e.scalar_tensor_tensor(
            kf_b[j][:], relu_b[j][:], 1.0, exp_b[j][:], ALU.add, ALU.min),
             reads=[("relu", j, 0), ("relu", j, 1), ("exp", j, 0), ("exp", j, 1)],
             writes=[("kf", j)])
        for c in range(NCH):
            p.op("tensor", lambda e, cc=c, tt=t: e.matmul(
                PS_A[:, KVB[cc]:KVB[cc] + 129],
                kf_b[j][:, cc * 128:(cc + 1) * 128],
                vext_b[j][:, cc * 129:(cc + 1) * 129],
                start=(tt == 0 and cc in (0, 3, 6)), stop=(tt == NT - 1),
                skip_group_check=True),
                 reads=[("kf", j), ("vext", j, 0), ("vext", j, 1), ("vext1", j)],
                 writes=[("PA",), ("PDS",)])

    def run_pass1(seq, fold_steps=None):
        xk = {0: load_x(seq, 0), 1: load_x(seq, 1)}
        for slot in range(NT + 1):
            t, s = slot, slot - 1
            if t < NT:
                if t + 2 < NT:
                    xk[t + 2] = load_x(seq, t + 2)
                p1_stats(seq, t, xk[t])
                p1_apply(seq, t, xk[t])
            if s >= 0:
                p1_kv(seq, s)

    def kv_evac():
        p.op("vector", lambda e: e.memset(kvblk_b[:], 0.0), writes=[("kvblk",)])
        p.op("vector", lambda e: e.memset(zblk_b[:], 0.0), writes=[("zblk",)])
        for g, (b0, cnt) in enumerate(((0, 3), (512, 3), (1024, 2))):
            dst0 = g * 3 * 128
            p.op("vector", lambda e, bb=b0, dd=dst0, cc=cnt: e.tensor_copy(
                bass.AP(kvblk_b, dd, [[1024, 64], [128, cc], [1, 64]]),
                bass.AP(PS_A, bb, [[1536, 64], [129, cc], [1, 64]])),
                 reads=[("PA",), ("PDS",)], writes=[("kvblk",)])
            p.op("vector", lambda e, bb=b0, dd=dst0, cc=cnt: e.tensor_copy(
                bass.AP(kvblk_b, 64 * 1024 + dd + 64, [[1024, 64], [128, cc], [1, 64]]),
                bass.AP(PS_A, 64 * 1536 + bb + 64, [[1536, 64], [129, cc], [1, 64]])),
                 reads=[("PA",), ("PDS",)], writes=[("kvblk",)])
            p.op("vector", lambda e, bb=b0, gg=g, cc=cnt: e.tensor_copy(
                bass.AP(zblk_b, gg * 3 * 18, [[128, 64], [18, cc]]),
                bass.AP(PS_A, bb + 128, [[1536, 64], [129, cc]])),
                 reads=[("PA",), ("PDS",)], writes=[("zblk",)])
            p.op("vector", lambda e, bb=b0, gg=g, cc=cnt: e.tensor_copy(
                bass.AP(zblk_b, 64 * 128 + gg * 3 * 18 + 1, [[128, 64], [18, cc]]),
                bass.AP(PS_A, 64 * 1536 + bb + 128, [[1536, 64], [129, cc]])),
                 reads=[("PA",), ("PDS",)], writes=[("zblk",)])

    def fold():
        for jc in range(NCH):
            cl = new_cluster()
            for hh in range(2):
                for k in range(NCH):
                    p.op("tensor", lambda e, m=jc, h2=hh, kk=k: e.matmul(
                        PS_C[:, h2 * 512:(h2 + 1) * 512],
                        a1_b[:, kk * 1024 + m * 128: kk * 1024 + m * 128 + 128],
                        wt_b[:, kk * 1024 + h2 * 512: kk * 1024 + (h2 + 1) * 512],
                        start=(kk == 0), stop=(kk == NCH - 1), skip_group_check=True),
                         reads=A1K + WTK, writes=[("PC",)], cluster=cl)
            p.op("vector", lambda e, m=jc: e.tensor_copy(
                qt_b[:, m * 1024:(m + 1) * 1024], PS_C[:]),
                 reads=[("PC",)], writes=[("qt", jc)])
        QTK = [("qt", k) for k in range(NCH)]
        for ic in range(NCH):
            cl = new_cluster()
            for hh in range(2):
                for k in range(NCH):
                    p.op("tensor", lambda e, m=ic, h2=hh, kk=k: e.matmul(
                        PS_C[:, h2 * 512:(h2 + 1) * 512],
                        qt_b[:, kk * 1024 + m * 128: kk * 1024 + m * 128 + 128],
                        a2_b[:, kk * 1024 + h2 * 512: kk * 1024 + (h2 + 1) * 512],
                        start=(kk == 0), stop=(kk == NCH - 1), skip_group_check=True),
                         reads=QTK + A2K, writes=[("PC",)], cluster=cl)
            gate_w = ([("p_b",)] if ic < NCH - 1 else
                      [("p_b",), ("wt", 0), ("wt", 1), ("wt", 2), ("wt", 3),
                       ("a1", 0), ("a1", 1), ("a1", 2), ("a1", 3),
                       ("a2", 0), ("a2", 1), ("a2", 2), ("a2", 3)]
                      + [("qt", k) for k in range(NCH)]
                      + [("stg", i) for i in range(6)]
                      + [("stageA", 0), ("stageA", 1), ("stageB", 0),
                         ("stageB", 1), ("foldgate",)])
            p.op("vector", lambda e, m=ic: e.tensor_copy(
                p_b[:, m * 1024:(m + 1) * 1024], PS_C[:]),
                 reads=[("PC",)], writes=gate_w)

    def gate0(seq, t):
        return [("foldgate",)] if (seq == 0 and t < 2) else []

    def p2_A1(seq, t):
        g0 = gate0(seq, t)
        j = t % 2
        cl = new_cluster()
        for c in range(NCH):
            p.op("tensor", lambda e, cc=c, tt=t: e.matmul(
                PS_A[:, cc * 128:(cc + 1) * 128],
                wq_b[:, cc * 128:(cc + 1) * 128], xslice(cc, tt),
                start=True, stop=True, skip_group_check=True),
                 reads=[("xnc", t), ("wq",)], writes=[("PA",)], cluster=cl)
        p.op("scalar", lambda e: e.activation(relu_b[j][:], PS_A[:, 0:1024], AF.Relu),
             reads=[("PA",)], writes=[("relu", j, 0), ("relu", j, 1)])
        p.op("scalar", lambda e: e.activation(exp_b[j][:], PS_A[:, 0:1024], AF.Exp),
             reads=[("PA",)], writes=[("exp", j, 0), ("exp", j, 1)])
        p.op("vector", lambda e: e.scalar_tensor_tensor(
            qf_b[j], relu_b[j][:], 1.0, exp_b[j][:], ALU.add, ALU.min),
             reads=[("relu", j, 0), ("relu", j, 1), ("exp", j, 0), ("exp", j, 1)] + g0,
             writes=[("qf", j)])

    def p2_A2(seq, t):
        g0 = gate0(seq, t)
        j = t % 2
        cl = new_cluster()
        for c in range(NCH):
            p.op("tensor", lambda e, cc=c: e.matmul(
                PS_A[0:16, 1024:1152], zblk_b[:, cc * 16:(cc + 1) * 16],
                qf_b[j][:, cc * 128:(cc + 1) * 128],
                start=(cc == 0), stop=(cc == NCH - 1), skip_group_check=True),
                 reads=[("qf", j), ("zblk",)], writes=[("PDS",)], cluster=cl)
        p.op("vector", lambda e: e.tensor_scalar_add(
            den_f[j][:], PS_A[0:16, 1024:1152], float(EPS_ATTN)),
             reads=[("PDS",)], writes=[("den", j)])
        orc = p.op("vector", lambda e: e.reciprocal(rden_f[j][:], den_f[j][:]),
                   reads=[("den", j)], writes=[("rdenf", j)])
        p.op("vector", lambda e: e.tensor_copy(rden_b[j][:], rden_f[j][:]),
             reads=[("rdenf", j)], writes=[("rdenb", j)], force=[orc])
        for hh in range(2):
            cl = new_cluster()
            for c4 in range(4):
                cc0 = hh * 4 + c4
                p.op("tensor", lambda e, cc=cc0, c2=c4: e.matmul(
                    PS_A[:, 1024 + c2 * 128:1024 + (c2 + 1) * 128],
                    sel_b[:, cc * 128:(cc + 1) * 128], rden_b[j][:],
                    start=True, stop=True, skip_group_check=True),
                     reads=[("rdenb", j), ("sel",)], writes=[("PDS",)], cluster=cl)
            p.op("vector", lambda e, h2=hh: e.tensor_copy(
                rdenbc_b[j][:, h2 * 512:(h2 + 1) * 512], PS_A[:, 1024:1536]),
                 reads=[("PDS",)] + g0, writes=[("rdenbc", j, hh)])

        p.op("gpsimd", lambda e: e.tensor_tensor(
            qf2_b[j], qf_b[j], rdenbc_b[j], ALU.mult),
             reads=[("qf", j), ("rdenbc", j, 0), ("rdenbc", j, 1)],
             writes=[("qf2", j)])
        for hh in range(2):
            cl = new_cluster()
            for c4 in range(4):
                cc0 = hh * 4 + c4
                p.op("tensor", lambda e, cc=cc0, c2=c4: e.matmul(
                    PS_A[:, 1024 + c2 * 128:1024 + (c2 + 1) * 128],
                    kvblk_b[:, cc * 128:(cc + 1) * 128],
                    qf2_b[j][:, cc * 128:(cc + 1) * 128],
                    start=True, stop=True, skip_group_check=True),
                     reads=[("qf2", j), ("kvblk",)], writes=[("PDS",)], cluster=cl)
            p.op("scalar", lambda e, h2=hh: e.copy(
                ctx_b[j][:, h2 * 512:(h2 + 1) * 512], PS_A[:, 1024:1536]),
                 reads=[("PDS",)] + g0, writes=[("ctx", j, hh)])


    def p2_A2b(seq, t):
        j = t % 2
        cl = new_cluster()
        for hh in range(2):
            for c in range(NCH):
                p.op("tensor", lambda e, cc=c, h2=hh: e.matmul(
                    PS_C[:, h2 * 512:(h2 + 1) * 512],
                    ctx_b[j][:, cc * 128:(cc + 1) * 128],
                    wo_b[:, cc * 1024 + h2 * 512: cc * 1024 + (h2 + 1) * 512],
                    start=(cc == 0), stop=(cc == NCH - 1), skip_group_check=True),
                     reads=[("ctx", j, 0), ("ctx", j, 1)] + WOK,
                     writes=[("PC",)], cluster=cl)

    def p2_B(seq, t, xk):
        g0 = gate0(seq, t)
        j = t % 2
        xbuf, xkey = xk
        ln_stats(st2[j], ("s2", j), PS_C[:], [("PC",)])
        ln_rsqrt(st2[j], ("s2", j), EPS_LN)
        ln_mb(st2[j], ("s2", j))
        ln_apply_s(st2[j], ("s2", j), lnS[j], PS_C[:], [("PC",)] + g0, [("lnS", j)])
        p.op("vector", lambda e: e.tensor_add(x1_b[j], xbuf[:], lnS[j]),
             reads=[xkey, ("lnS", j)] + g0, writes=[("x1", j)])
        ln_stats(st3[j], ("s3", j), x1_b[j], [("x1", j)])
        ln_rsqrt(st3[j], ("s3", j), EPS_LN)
        ln_mb(st3[j], ("s3", j))
        ln_apply_v(st3[j], ("s3", j), xn2tm_b[j], x1_b[j], [("x1", j)] + g0,
                   [("xn2tm", j)])
        cl = new_cluster()
        for c in range(NCH):
            p.op("tensor", lambda e, cc=c: e.transpose(
                PS_T[:, cc * 128:(cc + 1) * 128],
                xn2tm_b[j][:, cc * 128:(cc + 1) * 128], eye_b[:]),
                 reads=[("xn2tm", j), ("eye",)], writes=[("PT",)], cluster=cl)
        p.op("vector", lambda e: e.tensor_copy(xn2f_b[j], PS_T[:]),
             reads=[("PT",)] + g0, writes=[("xn2f", j)])
        cl = new_cluster()
        for hh in range(2):
            p.op("tensor", lambda e, h2=hh: e.matmul(
                PS_B[:, h2 * 512:(h2 + 1) * 512], onesr_b[0:1, :],
                bias_b[0:1, h2 * 512:(h2 + 1) * 512],
                start=True, stop=False, skip_group_check=True),
                 reads=[("onesr",), ("biasb",)], writes=[("PB",)], cluster=cl)
        for hh in range(2):
            for c in range(NCH):
                p.op("tensor", lambda e, cc=c, h2=hh: e.matmul(
                    PS_B[:, h2 * 512:(h2 + 1) * 512],
                    xn2f_b[j][:, cc * 128:(cc + 1) * 128],
                    p_b[:, cc * 1024 + h2 * 512: cc * 1024 + (h2 + 1) * 512],
                    start=False, stop=(cc == NCH - 1), skip_group_check=True),
                     reads=[("xn2f", j), ("p_b",)], writes=[("PB",)], cluster=cl)
        p.op("scalar", lambda e: e.activation(t_b[j], PS_B[:], AF.Exp, scale=2.0),
             reads=[("PB",)] + g0, writes=[("t_b", j)])

    def p2_C(seq, t):
        g0 = gate0(seq, t)
        j = t % 2
        p.op("scalar", lambda e: e.activation(t_b[j], t_b[j], AF.Ln, bias=one_s[:]),
             reads=[("t_b", j), ("one",)], writes=[("t_b", j)])
        p.op("scalar", lambda e: e.activation(s_b[j], t_b[j], AF.Exp, scale=-1.0),
             reads=[("t_b", j)] + g0, writes=[("s_b", j)])
        ln_stats(st4[j], ("s4", j), s_b[j], [("s_b", j)])
        ln_rsqrt(st4[j], ("s4", j), EPS_LN / 4.0)
        ln_mb(st4[j], ("s4", j))
        ln_apply_v(st4[j], ("s4", j), lnS4[j], s_b[j], [("s_b", j)] + g0,
                   [("lnS4", j)])
        p.op("gpsimd", lambda e: e.tensor_tensor(
            out_b[j][:], x1_b[j], lnS4[j], ALU.subtract),
             reads=[("x1", j), ("lnS4", j)], writes=[("out", j)])
        row0 = seq * SEQLEN + t * 128
        dma("gpsimd", y_d[row0:row0 + 128, :], out_b[j][:], reads=[("out", j)])

    def run_pass2(seq):
        xk = {0: load_x(seq, 0), 1: load_x(seq, 1)}
        for slot in range(NT + 2):
            tA, tB, tC = slot, slot - 1, slot - 2
            if tA < NT:
                if tA + 2 < NT:
                    xk[tA + 2] = load_x(seq, tA + 2)
                p2_A1(seq, tA)
                p2_A2(seq, tA)
            if 0 <= tB < NT:
                p2_B(seq, tB, xk[tB])
            if tA < NT:
                p2_A2b(seq, tA)
            if 0 <= tC < NT:
                p2_C(seq, tC)

    run_pass1(0)
    kv_evac()
    fold()
    run_pass2(0)
    run_pass1(1)
    kv_evac()
    run_pass2(1)

    import os
    if os.environ.get("KSCHED", "1") == "1":
        p.schedule()
    p.resolve()
    build_nc.last_plan = p

    with ExitStack() as sst:
        sems = {name: sst.enter_context(nc.semaphore(name))
                for name in p.final_counts}
        with nc.allow_low_precision("deliberate bf16 matmul-operand casts; "
                                    "fp32 accumulation and elementwise math"):
            with nc.Block() as block:

                @block.sync
                def _(eng):
                    p.emit_engine("sync", eng, sems)
                    for name, total in sorted(p.final_counts.items()):
                        if name.startswith("dma_"):
                            eng.wait_ge(sems[name], total)

                @block.tensor
                def _(eng):
                    p.emit_engine("tensor", eng, sems)

                @block.vector
                def _(eng):
                    p.emit_engine("vector", eng, sems)

                @block.scalar
                def _(eng):
                    p.emit_engine("scalar", eng, sems)

                @block.gpsimd
                def _(eng):
                    p.emit_engine("gpsimd", eng, sems)

    return nc


def kernel(L0_attn_query, L0_attn_Wq, L0_attn_Wk, L0_attn_Wv, L0_attn_Wo,
           L0_ffn_weight, L0_ffn_aux_weight1, L0_ffn_aux_weight2, L0_ffn_bias):
    f32 = np.float32
    x = np.ascontiguousarray(np.asarray(L0_attn_query, dtype=f32))
    Wq = np.asarray(L0_attn_Wq, dtype=f32)
    Wk = np.asarray(L0_attn_Wk, dtype=f32)
    Wv = np.asarray(L0_attn_Wv, dtype=f32)

    wqblk = np.zeros((NCH, 128, 128), dtype=f32)
    wkvblk = np.zeros((NCH, 128, 256), dtype=f32)
    for c in range(NCH):
        h0, h1 = 2 * c, 2 * c + 1
        wqblk[c, 0:64, 0:64] = Wq[h0]
        wqblk[c, 64:128, 64:128] = Wq[h1]
        wkvblk[c, 0:64, 0:64] = Wk[h0]
        wkvblk[c, 64:128, 64:128] = Wk[h1]
        wkvblk[c, 0:64, 128:192] = Wv[h0]
        wkvblk[c, 64:128, 192:256] = Wv[h1]
    sel = np.zeros((16, NCH * 128), dtype=f32)
    for c in range(NCH):
        sel[2 * c, c * 128:c * 128 + 64] = 1.0
        sel[2 * c + 1, c * 128 + 64:c * 128 + 128] = 1.0
    common = {
        "wqblk": wqblk,
        "wkvblk": wkvblk,
        "wo": np.ascontiguousarray(np.asarray(L0_attn_Wo, dtype=f32)),
        "wt": np.ascontiguousarray(np.asarray(L0_ffn_weight, dtype=f32).T),
        "a1": np.ascontiguousarray(np.asarray(L0_ffn_aux_weight1, dtype=f32)),
        "a2": np.ascontiguousarray(np.asarray(L0_ffn_aux_weight2, dtype=f32)),
        "bias": np.ascontiguousarray(
            np.asarray(L0_ffn_bias, dtype=f32).reshape(1, D)),
        "sel": sel,
        "eye": np.eye(128, dtype=f32),
    }
    B = x.shape[0]
    xr = x.reshape(B, SEQLEN, D)
    in_maps = []
    for c in range(NCORES):
        xc = np.ascontiguousarray(
            xr[c * SEQ_PER_CORE:(c + 1) * SEQ_PER_CORE].reshape(ROWS, D))
        in_maps.append({"x": xc, **common})

    nc = build_nc()
    res = run_bass_kernel_spmd(nc, in_maps, core_ids=list(range(NCORES)))
    out = np.empty((B, SEQLEN, D), dtype=np.float32)
    for c in range(NCORES):
        out[c * SEQ_PER_CORE:(c + 1) * SEQ_PER_CORE] = \
            res.results[c]["y"].reshape(SEQ_PER_CORE, SEQLEN, D)
    return out.reshape(x.shape)


# version 19 (final)
# speedup vs baseline: 1.0162x; 1.0162x over previous
import numpy as np
import concourse.bass as bass
import concourse.mybir as mybir
from concourse.bass_utils import run_bass_kernel_spmd
from contextlib import ExitStack

F32 = mybir.dt.float32
BF16 = mybir.dt.bfloat16
AF = mybir.ActivationFunctionType
ALU = mybir.AluOpType

D = 1024
NCH = 8
NCORES = 8
SEQ_PER_CORE = 2
SEQLEN = 2048
NT = SEQLEN // 128
ROWS = SEQ_PER_CORE * SEQLEN
EPS_LN = 1e-5
EPS_ATTN = 1e-6

KVB = [0, 129, 258, 512, 641, 770, 1024, 1153]



class _FakeInst:
    def then_inc(self, *a, **k):
        return self


class _FakeEng:
    def __init__(self):
        self.calls = []

    def __getattr__(self, name):
        def rec(*args, **kwargs):
            shapes = []
            for a in list(args) + list(kwargs.values()):
                sh = getattr(a, 'shape', None)
                if sh is not None:
                    shapes.append(tuple(sh))
            self.calls.append((name, shapes))
            return _FakeInst()
        return rec


def _op_cost(op):
    f = _FakeEng()
    try:
        op["fn"](f)
        name, shapes = f.calls[0]
    except Exception:
        return 500.0
    eng = op["engine"]

    def nel(sh):
        n = 1
        for d in sh[1:]:
            n *= d
        return n
    if name == "matmul":
        N = nel(shapes[0])
        P = nel(shapes[1])
        return max(N / 2.4, P / 1.2) + 60
    if name == "transpose":
        return 275.0
    if name == "dma_start":
        n = 1
        for d in shapes[0]:
            n *= d
        return 1700.0 + n * 4 / 55.0
    if name == "bn_stats":
        return 680.0
    if name == "bn_aggr":
        return 210.0
    if name == "reciprocal":
        return 860.0
    if name in ("memset", "nop"):
        return 120.0
    n = nel(shapes[0]) if shapes else 128
    if name == "activation":
        return (200 + n) / 1.2
    if name == "copy":
        return (172 + n) / 1.2
    if name == "tensor_copy":
        return 2.9 * n if eng == "gpsimd" else (120 + n / 2) / 0.96 + 60
    if name in ("tensor_tensor", "tensor_add", "tensor_sub"):
        return 2.2 * n if eng == "gpsimd" else (151 + n) / 0.96 + 60
    if name == "scalar_tensor_tensor":
        return 2.2 * n if eng == "gpsimd" else (151 + n) / 0.96 + 600
    if name in ("tensor_scalar", "tensor_scalar_add"):
        return 1.5 * n if eng == "gpsimd" else (58 + n / 2) / 0.96 + 60
    return 500.0


class Plan:

    DMA_ROT = 3

    def __init__(self):
        self.ops = []
        self.keys = {}
        self.dma_hist = {}

    def op(self, engine, fn, reads=(), writes=(), dma=False, force=(), cost=500,
           cluster=None):
        oid = len(self.ops)
        d = set()
        for k in reads:
            st = self.keys.get(k)
            if st:
                d.update(st["w"])
        for k in writes:
            st = self.keys.get(k)
            if st:
                d.update(st["w"])
                d.update(st["r"])
        for k in reads:
            self.keys.setdefault(k, {"w": [], "r": []})["r"].append(oid)
        for k in writes:
            st = self.keys.setdefault(k, {"w": [], "r": []})
            st["w"] = [oid]
            st["r"] = []
        d.discard(oid)
        d.update(force)
        sem = None
        if dma:
            hist = self.dma_hist.setdefault(engine, [])
            if len(hist) >= self.DMA_ROT:
                d.add(hist[len(hist) - self.DMA_ROT])
            sem = f"dma_{engine}{len(hist) % self.DMA_ROT}"
            hist.append(oid)
        self.ops.append({"engine": engine, "fn": fn, "deps": sorted(d),
                         "dma": dma, "inc": False, "tok": None, "sem": sem,
                         "force": set(force), "cost": cost,
                         "cluster": cluster})
        return oid

    def schedule(self):
        n = len(self.ops)
        cost = [_op_cost(o) for o in self.ops]
        succ = [[] for _ in range(n)]
        npred = [0] * n
        for i, o in enumerate(self.ops):
            for d in o["deps"]:
                succ[d].append(i)
                npred[i] += 1
        prio = [0.0] * n
        for i in range(n - 1, -1, -1):
            m = 0.0
            for j in succ[i]:
                if prio[j] > m:
                    m = prio[j]
            prio[i] = m + cost[i]
        import os
        W = int(os.environ.get("KSCHED_W", "2500"))
        LO = int(os.environ.get("KSCHED_LO", "0"))
        HI = int(os.environ.get("KSCHED_HI", str(n)))
        PIN = set(os.environ.get("KSCHED_PIN", "").split(",")) - {""}
        if PIN:
            last_pin = {}
            for i, o in enumerate(self.ops):
                tag = None
                if o["engine"] in PIN:
                    tag = o["engine"]
                if "dma" in PIN and o["dma"]:
                    tag = "dma_" + o["engine"]
                if tag is not None:
                    if tag in last_pin:
                        d0 = last_pin[tag]
                        if d0 not in o["deps"]:
                            o["deps"] = sorted(o["deps"] + [d0])
                    last_pin[tag] = i
        clusters = {}
        for i, o in enumerate(self.ops):
            if o["cluster"] is not None:
                clusters.setdefault(o["cluster"], []).append(i)
        cl_members = {}
        for cid, mem in clusters.items():
            for i in mem:
                cl_members[i] = mem
        eng_free = {}
        end = [0.0] * n
        ready = set(i for i in range(n) if npred[i] == 0)
        order = []
        dep_ready = [0.0] * n
        nsched = 0
        sched_mark = [False] * n
        frontier = 0
        while ready:
            while frontier < n and sched_mark[frontier]:
                frontier += 1
            best = None
            best_key = None
            best_st = None
            SLACK = float(os.environ.get("KSCHED_SLACK", "300"))
            cands = []
            for i in ready:
                if i != frontier and not (LO <= i < HI and LO <= frontier < HI
                                          and i <= frontier + W):
                    continue
                grp = cl_members.get(i)
                if grp is not None:
                    if i != grp[0]:
                        continue
                    gs = set(grp)
                    if not all(all((d in gs) or sched_mark[d]
                                   for d in self.ops[m]["deps"]) for m in grp):
                        continue
                e = self.ops[i]["engine"]
                st = max(eng_free.get(e, 0.0), dep_ready[i])
                cands.append((st, i))
                if best_st is None or st < best_st:
                    best_st = st
            for st, i in cands:
                if st <= best_st + SLACK:
                    key = (-prio[i], st)
                    if best_key is None or key < best_key:
                        best_key = key
                        best = i
            if best is None:
                best = frontier
            i = best
            group = cl_members.get(i, [i])
            for m in group:
                if sched_mark[m]:
                    continue
                ready.discard(m)
                sched_mark[m] = True
                om = self.ops[m]
                em = om["engine"]
                stm = max(eng_free.get(em, 0.0), dep_ready[m])
                end[m] = stm + cost[m]
                eng_free[em] = stm + (150.0 if om["dma"] else cost[m])
                order.append(m)
                for jj in succ[m]:
                    npred[jj] -= 1
                    tt = end[m] + (110.0 if em != self.ops[jj]["engine"] else 0.0)
                    if tt > dep_ready[jj]:
                        dep_ready[jj] = tt
                    if npred[jj] == 0 and not sched_mark[jj]:
                        ready.add(jj)
            continue
        assert len(order) == n
        remap = {old: new for new, old in enumerate(order)}
        new_ops = []
        for new, old in enumerate(order):
            o = self.ops[old]
            o["deps"] = sorted(remap[d] for d in o["deps"])
            o["force"] = set(remap[d] for d in o["force"])
            new_ops.append(o)
        self.ops = new_ops

    def resolve(self):
        n = len(self.ops)
        cnt0 = {}
        tok0 = [None] * n
        for i, o in enumerate(self.ops):
            sem = o["sem"] if o["dma"] else ("eng_" + o["engine"])
            amt = 16 if o["dma"] else 1
            cnt0[sem] = cnt0.get(sem, 0) + amt
            tok0[i] = (sem, cnt0[sem], amt)
        ic = {}
        cc = [None] * n
        waitlists = [None] * n
        waited = set()
        for i, o in enumerate(self.ops):
            e = o["engine"]
            myic = ic.setdefault(e, {})
            need = []
            for d in sorted(o["deps"], key=lambda d: -tok0[d][1]):
                dop = self.ops[d]
                if (e == "tensor" and dop["engine"] == "tensor"
                        and not dop["dma"] and not o["dma"]
                        and d not in o["force"]):
                    continue
                s, v, _ = tok0[d]
                if myic.get(s, 0) >= v:
                    continue
                need.append(d)
                waited.add(d)
                for s2, v2 in cc[d].items():
                    if myic.get(s2, 0) < v2:
                        myic[s2] = v2
                if myic.get(s, 0) < v:
                    myic[s] = v
            waitlists[i] = need
            mycc = dict(myic)
            s, v, _ = tok0[i]
            mycc[s] = max(mycc.get(s, 0), v)
            cc[i] = mycc
        counts = {}
        for i, o in enumerate(self.ops):
            o["inc"] = o["dma"] or (i in waited)
            if not o["inc"]:
                o["tok"] = None
                continue
            sem = o["sem"] if o["dma"] else ("eng_" + o["engine"])
            amt = 16 if o["dma"] else 1
            counts[sem] = counts.get(sem, 0) + amt
            o["tok"] = (sem, counts[sem], amt)
        self._waitlists = waitlists
        self.final_counts = counts

    def emit_engine(self, name, eng, sems):
        for i, o in enumerate(self.ops):
            if o["engine"] != name:
                continue
            for dep in self._waitlists[i]:
                sem, val, _ = self.ops[dep]["tok"]
                eng.wait_ge(sems[sem], val)
            inst = o["fn"](eng)
            if o["inc"]:
                sem, _, amt = o["tok"]
                inst.then_inc(sems[sem], amt)


def build_nc():
    nc = bass.Bass("TRN2", target_bir_lowering=False)

    x_d = nc.dram_tensor("x", [ROWS, D], F32, kind="ExternalInput").ap()
    wqblk_d = nc.dram_tensor("wqblk", [NCH, 128, 128], F32, kind="ExternalInput").ap()
    wkvblk_d = nc.dram_tensor("wkvblk", [NCH, 128, 256], F32, kind="ExternalInput").ap()
    wo_d = nc.dram_tensor("wo", [D, D], F32, kind="ExternalInput").ap()
    wt_d = nc.dram_tensor("wt", [D, D], F32, kind="ExternalInput").ap()
    a1_d = nc.dram_tensor("a1", [D, D], F32, kind="ExternalInput").ap()
    a2_d = nc.dram_tensor("a2", [D, D], F32, kind="ExternalInput").ap()
    bias_d = nc.dram_tensor("bias", [1, D], F32, kind="ExternalInput").ap()
    sel_d = nc.dram_tensor("sel", [16, NCH * 128], F32, kind="ExternalInput").ap()
    eye_d = nc.dram_tensor("eye", [128, 128], F32, kind="ExternalInput").ap()
    y_d = nc.dram_tensor("y", [ROWS, D], F32, kind="ExternalOutput").ap()

    A = nc.alloc_sbuf_tensor
    wq_b = A("wq_b", [128, NCH * 128], BF16)
    wkv_b = A("wkv_b", [128, NCH * 256], BF16)
    wo_b = A("wo_b", [128, NCH * 1024], BF16)
    p_b = A("p_b", [128, NCH * 1024], BF16)
    eye_b = A("eye_b", [128, 128], BF16)
    sel_b = A("sel_b", [16, NCH * 128], BF16)
    bias_b = A("bias_b", [1, D], BF16)
    onesr_b = A("onesr_b", [1, 128], BF16)
    one_s = A("one_s", [128, 1], F32)
    eps_s = A("eps_s", [128, 1], F32)
    eps4_s = A("eps4_s", [128, 1], F32)
    xn_cache = A("xn_cache", [128, NCH * SEQLEN], BF16)
    kvblk_b = A("kvblk_b", [128, NCH * 128], BF16)
    zblk_b = A("zblk_b", [128, NCH * 16], BF16)
    kf_b = [A(f"kf_b{i}", [128, D], BF16) for i in range(2)]
    vext_b = [A(f"vext_b{i}", [128, NCH * 129], BF16) for i in range(2)]
    xn_tm = [A(f"xn_tm{i}", [128, D], BF16) for i in range(2)]
    relu_b = [A(f"relu_b{i}", [128, D], BF16) for i in range(2)]
    exp_b = [A(f"exp_b{i}", [128, D], BF16) for i in range(2)]
    x_tm = [A(f"x_tm{i}", [128, D], F32) for i in range(4)]
    out_b = [A(f"out_b{i}", [128, D], F32) for i in range(2)]
    den_f = [A(f"den_f{i}", [16, 128], F32) for i in range(2)]
    rden_f = [A(f"rden_f{i}", [16, 128], F32) for i in range(2)]
    rden_b = [A(f"rden_b{i}", [16, 128], BF16) for i in range(2)]

    def statbufs(name, n=2):
        return [dict(
            bn=A(f"{name}_bn{i}", [128, 12], F32),
            ag=A(f"{name}_ag{i}", [128, 2], F32),
            ve=A(f"{name}_ve{i}", [128, 1], F32),
            l=A(f"{name}_l{i}", [128, 1], F32),
            r=A(f"{name}_r{i}", [128, 1], F32),
            mb=A(f"{name}_mb{i}", [128, 1], F32),
        ) for i in range(n)]
    st1 = statbufs("ln1")
    st2 = statbufs("ln2")
    st3 = statbufs("ln3")
    st4 = statbufs("ln4")

    big_b = A("big_b", [128, 32768], BF16)
    wt_b = big_b[:, 0:8192]
    a1_b = big_b[:, 8192:16384]
    a2_b = big_b[:, 16384:24576]
    qt_b = big_b[:, 24576:32768]
    bigf = big_b.bitcast(F32)
    bslc = lambda i: big_b[:, 1024 * i:1024 * (i + 1)]
    fslc = lambda i: bigf[:, 1024 * i:1024 * (i + 1)]
    qf_b = [bslc(0), bslc(1)]
    qf2_b = [bslc(2), bslc(3)]
    rdenbc_b = [bslc(4), bslc(5)]
    ctx_b = [bslc(6), bslc(7)]
    xn2tm_b = [bslc(8), bslc(9)]
    xn2f_b = [bslc(10), bslc(11)]
    lnS = [fslc(6), fslc(7)]
    lnS4 = [fslc(8), fslc(9)]
    x1_b = [fslc(10), fslc(11)]
    t_b = [fslc(12), fslc(13)]
    s_b = [fslc(14), fslc(15)]
    stage = [A("stage0", [128, 2048], F32)[:], A("stage1", [128, 2048], F32)[:]]
    rem = nc.sbuf_bytes_remaining
    assert rem >= 0, f"SBUF over budget by {-rem} bytes"

    PS_A = nc.alloc_psum_tensor("PS_A", [128, 1536], F32)
    PS_B = nc.alloc_psum_tensor("PS_B", [128, 1024], F32)
    PS_C = nc.alloc_psum_tensor("PS_C", [128, 1024], F32)
    PS_T = nc.alloc_psum_tensor("PS_T", [128, 1024], BF16)

    p = Plan()
    _clc = [0]

    def new_cluster():
        _clc[0] += 1
        return _clc[0]

    def dma(engine, out_ap, in_ap, reads=(), writes=()):
        return p.op(engine, lambda e, o=out_ap, i=in_ap: e.dma_start(out=o, in_=i),
                    reads=reads, writes=writes, dma=True)

    dma("sync", stage[0][:, 0:1024].rearrange("p (a b) -> p a b", a=NCH),
        wqblk_d.rearrange("a p b -> p a b"), writes=[("stg", 0)])
    p.op("gpsimd", lambda e: e.tensor_copy(wq_b[:], stage[0][:, 0:1024]),
         reads=[("stg", 0)], writes=[("wq",)])
    dma("sync", stage[0][:, 1024:2048].rearrange("p (a b) -> p a b", a=4),
        wkvblk_d.rearrange("a p b -> p a b")[:, 0:4, :], writes=[("stg", 1)])
    p.op("gpsimd", lambda e: e.tensor_copy(
        wkv_b[:, 0:1024].rearrange("p (a b) -> p a b", a=4),
        stage[0][:, 1024:2048].rearrange("p (a b) -> p a b", a=4)),
         reads=[("stg", 1)], writes=[("wkv", 0)])
    dma("sync", stage[1][:, 0:1024].rearrange("p (a b) -> p a b", a=4),
        wkvblk_d.rearrange("a p b -> p a b")[:, 4:8, :], writes=[("stg", 2)])
    p.op("gpsimd", lambda e: e.tensor_copy(
        wkv_b[:, 1024:2048].rearrange("p (a b) -> p a b", a=4),
        stage[1][:, 0:1024].rearrange("p (a b) -> p a b", a=4)),
         reads=[("stg", 2)], writes=[("wkv", 1)])
    dma("sync", stage[1][:, 1024:1152], eye_d, writes=[("stg", 3)])
    p.op("gpsimd", lambda e: e.tensor_copy(eye_b[:], stage[1][:, 1024:1152]),
         reads=[("stg", 3)], writes=[("eye",)])
    dma("sync", stage[1][0:16, 0:1024], sel_d, reads=[("wkv", 1)],
        writes=[("stg", 4)])
    p.op("gpsimd", lambda e: e.tensor_copy(sel_b[:], stage[1][0:16, 0:1024]),
         reads=[("stg", 4)], writes=[("sel",)])
    dma("sync", stage[0][0:1, 0:1024], bias_d, reads=[("wq",)], writes=[("stg", 5)])
    p.op("gpsimd", lambda e: e.tensor_copy(bias_b[:], stage[0][0:1, 0:1024]),
         reads=[("stg", 5)], writes=[("biasb",)])
    p.op("vector", lambda e: e.memset(one_s[:], 1.0), writes=[("one",)])
    p.op("vector", lambda e: e.memset(eps_s[:], float(EPS_LN)), writes=[("eps",)])
    p.op("vector", lambda e: e.memset(eps4_s[:], float(EPS_LN / 4.0)), writes=[("eps4",)])
    p.op("vector", lambda e: e.memset(onesr_b[:], 1.0), writes=[("onesr",)])
    for i in range(2):
        vv = vext_b[i].rearrange("p (c n) -> p c n", n=129)
        p.op("vector", lambda e, v3=vv: e.memset(v3[:, :, 128:129], 1.0),
             writes=[("vext1", i)])

    def big_load(queue, src, dst, key, sslot, skeyb, first_reads):
        for h in range(8):
            half = h % 2
            sl = sslot[:, half * 1024:(half + 1) * 1024]
            skey = (skeyb, half)
            dma(queue, sl.rearrange("p (a b) -> p a b", a=1),
                src.rearrange("(c p) n -> p c n", p=128)[:, h:h + 1, :],
                reads=first_reads if h < 2 else (), writes=[skey])
            p.op("gpsimd",
                 lambda e, d2=dst, s2=sl, hh=h: e.tensor_copy(
                     d2[:, hh * 1024:(hh + 1) * 1024], s2),
                 reads=[skey], writes=[(key, h // 2)])
    big_load("scalar", wt_d, wt_b, "wt", stage[0][:, 0:2048], "stageA",
             [("wq",), ("wkv", 0), ("biasb",)])
    big_load("scalar", wo_d, wo_b, "wo", stage[0][:, 0:2048], "stageA", [])
    big_load("gpsimd", a1_d, a1_b, "a1", stage[1][:, 0:2048], "stageB",
             [("wkv", 1), ("eye",), ("sel",)])
    big_load("gpsimd", a2_d, a2_b, "a2", stage[1][:, 0:2048], "stageB", [])
    WOK = [("wo", h) for h in range(4)]
    WTK = [("wt", h) for h in range(4)]
    A1K = [("a1", h) for h in range(4)]
    A2K = [("a2", h) for h in range(4)]

    nload = [0]

    def load_x(seq, t):
        i = nload[0] % 4
        nload[0] += 1
        buf, key = x_tm[i], ("x_tm", i)
        row0 = seq * SEQLEN + t * 128
        dma("sync", buf[:], x_d[row0:row0 + 128, :], writes=[key])
        return buf, key

    def ln_stats(sb, skey, src_ap, src_keys):
        p.op("vector", lambda e, a=src_ap: e.bn_stats(sb["bn"][:, 0:6], a[:, 0:512]),
             reads=src_keys, writes=[(skey, "bn0")])
        o1 = p.op("vector", lambda e, a=src_ap: e.bn_stats(sb["bn"][:, 6:12], a[:, 512:1024]),
                  reads=src_keys, writes=[(skey, "bn1")])
        o2 = p.op("vector", lambda e: e.bn_aggr(sb["ag"][:], sb["bn"][:]),
                  reads=[(skey, "bn0"), (skey, "bn1")], writes=[(skey, "ag")],
                  force=[o1])
        sb["_aggr_oid"] = o2

    def ln_rsqrt(sb, skey, eps):
        eb, ek = (eps4_s, ("eps4",)) if eps < EPS_LN else (eps_s, ("eps",))
        p.op("scalar", lambda e, b2=eb: e.activation(
            sb["l"][:], sb["ag"][:, 1:2], AF.Ln, bias=b2[:]),
             reads=[(skey, "ag"), ek], writes=[(skey, "l")])
        p.op("scalar", lambda e: e.activation(sb["r"][:], sb["l"][:], AF.Exp, scale=-0.5),
             reads=[(skey, "l")], writes=[(skey, "r")])

    def ln_mb(sb, skey):
        p.op("vector", lambda e: e.tensor_scalar(
            sb["mb"][:], sb["ag"][:, 0:1], sb["r"][:], -1.0, ALU.mult, ALU.mult),
             reads=[(skey, "ag"), (skey, "r")], writes=[(skey, "mb")])

    def ln_apply_v(sb, skey, out_ap, in_ap, in_keys, out_keys):
        p.op("vector", lambda e, o=out_ap, a=in_ap: e.tensor_scalar(
            o, a, sb["r"][:], sb["mb"][:], ALU.mult, ALU.add),
             reads=list(in_keys) + [(skey, "r"), (skey, "mb")], writes=out_keys)

    def ln_apply_s(sb, skey, out_ap, in_ap, in_keys, out_keys):
        p.op("scalar", lambda e, o=out_ap, a=in_ap: e.activation(
            o, a, AF.Identity, bias=sb["mb"][:], scale=sb["r"][:]),
             reads=list(in_keys) + [(skey, "r"), (skey, "mb")], writes=out_keys)

    xn3 = xn_cache.rearrange("p (c n) -> p c n", c=NCH)

    def xslice(c, t):
        return xn_cache[:, c * SEQLEN + t * 128: c * SEQLEN + (t + 1) * 128]

    def p1_stats(seq, t, xk):
        j = t % 2
        xbuf, xkey = xk
        ln_stats(st1[j], ("s1", j), xbuf[:], [xkey])
        ln_rsqrt(st1[j], ("s1", j), EPS_LN)

    def p1_apply(seq, t, xk):
        j = t % 2
        xbuf, xkey = xk
        ln_mb(st1[j], ("s1", j))
        ln_apply_s(st1[j], ("s1", j), xn_tm[j][:], xbuf[:], [xkey], [("xn_tm", j)])
        cl = new_cluster()
        for c in range(NCH):
            p.op("tensor", lambda e, cc=c: e.transpose(
                PS_T[:, cc * 128:(cc + 1) * 128],
                xn_tm[j][:, cc * 128:(cc + 1) * 128], eye_b[:]),
                 reads=[("xn_tm", j), ("eye",)], writes=[("PT",)], cluster=cl)
        p.op("scalar", lambda e, tt=t: e.copy(
            xn3[:, :, tt * 128:(tt + 1) * 128],
            PS_T[:].rearrange("p (c n) -> p c n", c=NCH)),
             reads=[("PT",)], writes=[("xnc", t)])

    def p1_kv(seq, t):
        j = t % 2
        cl = new_cluster()
        for c in range(NCH):
            ps = PS_B if c < 4 else PS_C
            pk = ("PB",) if c < 4 else ("PC",)
            wk = ("wkv", 0) if c < 4 else ("wkv", 1)
            p.op("tensor", lambda e, cc=c, tt=t, pp=ps: e.matmul(
                pp[:, (cc % 4) * 256:(cc % 4) * 256 + 256], xslice(cc, tt),
                wkv_b[:, cc * 256:(cc + 1) * 256],
                start=True, stop=True, skip_group_check=True),
                 reads=[("xnc", t), wk], writes=[pk], cluster=cl)
        kview_B = bass.AP(PS_B, 0, [[1024, 128], [256, 4], [1, 128]])
        kview_C = bass.AP(PS_C, 0, [[1024, 128], [256, 4], [1, 128]])
        vview_B = bass.AP(PS_B, 128, [[1024, 128], [256, 4], [1, 128]])
        vview_C = bass.AP(PS_C, 128, [[1024, 128], [256, 4], [1, 128]])
        r3 = relu_b[j][:].rearrange("p (c n) -> p c n", n=128)
        e3 = exp_b[j][:].rearrange("p (c n) -> p c n", n=128)
        p.op("scalar", lambda e: e.activation(r3[:, 0:4, :], kview_B, AF.Relu),
             reads=[("PB",)], writes=[("relu", j, 0)])
        p.op("scalar", lambda e: e.activation(r3[:, 4:8, :], kview_C, AF.Relu),
             reads=[("PC",)], writes=[("relu", j, 1)])
        p.op("scalar", lambda e: e.activation(e3[:, 0:4, :], kview_B, AF.Exp),
             reads=[("PB",)], writes=[("exp", j, 0)])
        p.op("scalar", lambda e: e.activation(e3[:, 4:8, :], kview_C, AF.Exp),
             reads=[("PC",)], writes=[("exp", j, 1)])
        v3 = vext_b[j].rearrange("p (c n) -> p c n", n=129)
        p.op("scalar", lambda e: e.copy(v3[:, 0:4, 0:128], vview_B),
             reads=[("PB",)], writes=[("vext", j, 0)])
        p.op("scalar", lambda e: e.copy(v3[:, 4:8, 0:128], vview_C),
             reads=[("PC",)], writes=[("vext", j, 1)])
        p.op("vector", lambda e: e.scalar_tensor_tensor(
            kf_b[j][:], relu_b[j][:], 1.0, exp_b[j][:], ALU.add, ALU.min),
             reads=[("relu", j, 0), ("relu", j, 1), ("exp", j, 0), ("exp", j, 1)],
             writes=[("kf", j)])
        for c in range(NCH):
            p.op("tensor", lambda e, cc=c, tt=t: e.matmul(
                PS_A[:, KVB[cc]:KVB[cc] + 129],
                kf_b[j][:, cc * 128:(cc + 1) * 128],
                vext_b[j][:, cc * 129:(cc + 1) * 129],
                start=(tt == 0 and cc in (0, 3, 6)), stop=(tt == NT - 1),
                skip_group_check=True),
                 reads=[("kf", j), ("vext", j, 0), ("vext", j, 1), ("vext1", j)],
                 writes=[("PA",), ("PDS",)])

    def run_pass1(seq, fold_steps=None):
        xk = {0: load_x(seq, 0), 1: load_x(seq, 1)}
        for slot in range(NT + 1):
            t, s = slot, slot - 1
            if t < NT:
                if t + 2 < NT:
                    xk[t + 2] = load_x(seq, t + 2)
                p1_stats(seq, t, xk[t])
                p1_apply(seq, t, xk[t])
            if s >= 0:
                p1_kv(seq, s)

    def kv_evac():
        p.op("vector", lambda e: e.memset(kvblk_b[:], 0.0), writes=[("kvblk",)])
        p.op("vector", lambda e: e.memset(zblk_b[:], 0.0), writes=[("zblk",)])
        for g, (b0, cnt) in enumerate(((0, 3), (512, 3), (1024, 2))):
            dst0 = g * 3 * 128
            p.op("vector", lambda e, bb=b0, dd=dst0, cc=cnt: e.tensor_copy(
                bass.AP(kvblk_b, dd, [[1024, 64], [128, cc], [1, 64]]),
                bass.AP(PS_A, bb, [[1536, 64], [129, cc], [1, 64]])),
                 reads=[("PA",), ("PDS",)], writes=[("kvblk",)])
            p.op("vector", lambda e, bb=b0, dd=dst0, cc=cnt: e.tensor_copy(
                bass.AP(kvblk_b, 64 * 1024 + dd + 64, [[1024, 64], [128, cc], [1, 64]]),
                bass.AP(PS_A, 64 * 1536 + bb + 64, [[1536, 64], [129, cc], [1, 64]])),
                 reads=[("PA",), ("PDS",)], writes=[("kvblk",)])
            p.op("vector", lambda e, bb=b0, gg=g, cc=cnt: e.tensor_copy(
                bass.AP(zblk_b, gg * 3 * 18, [[128, 64], [18, cc]]),
                bass.AP(PS_A, bb + 128, [[1536, 64], [129, cc]])),
                 reads=[("PA",), ("PDS",)], writes=[("zblk",)])
            p.op("vector", lambda e, bb=b0, gg=g, cc=cnt: e.tensor_copy(
                bass.AP(zblk_b, 64 * 128 + gg * 3 * 18 + 1, [[128, 64], [18, cc]]),
                bass.AP(PS_A, 64 * 1536 + bb + 128, [[1536, 64], [129, cc]])),
                 reads=[("PA",), ("PDS",)], writes=[("zblk",)])

    def fold():
        for jc in range(NCH):
            cl = new_cluster()
            for hh in range(2):
                for k in range(NCH):
                    p.op("tensor", lambda e, m=jc, h2=hh, kk=k: e.matmul(
                        PS_C[:, h2 * 512:(h2 + 1) * 512],
                        a1_b[:, kk * 1024 + m * 128: kk * 1024 + m * 128 + 128],
                        wt_b[:, kk * 1024 + h2 * 512: kk * 1024 + (h2 + 1) * 512],
                        start=(kk == 0), stop=(kk == NCH - 1), skip_group_check=True),
                         reads=A1K + WTK, writes=[("PC",)], cluster=cl)
            p.op("vector", lambda e, m=jc: e.tensor_copy(
                qt_b[:, m * 1024:(m + 1) * 1024], PS_C[:]),
                 reads=[("PC",)], writes=[("qt", jc)])
        QTK = [("qt", k) for k in range(NCH)]
        for ic in range(NCH):
            cl = new_cluster()
            for hh in range(2):
                for k in range(NCH):
                    p.op("tensor", lambda e, m=ic, h2=hh, kk=k: e.matmul(
                        PS_C[:, h2 * 512:(h2 + 1) * 512],
                        qt_b[:, kk * 1024 + m * 128: kk * 1024 + m * 128 + 128],
                        a2_b[:, kk * 1024 + h2 * 512: kk * 1024 + (h2 + 1) * 512],
                        start=(kk == 0), stop=(kk == NCH - 1), skip_group_check=True),
                         reads=QTK + A2K, writes=[("PC",)], cluster=cl)
            gate_w = ([("p_b",)] if ic < NCH - 1 else
                      [("p_b",), ("wt", 0), ("wt", 1), ("wt", 2), ("wt", 3),
                       ("a1", 0), ("a1", 1), ("a1", 2), ("a1", 3),
                       ("a2", 0), ("a2", 1), ("a2", 2), ("a2", 3)]
                      + [("qt", k) for k in range(NCH)]
                      + [("stg", i) for i in range(6)]
                      + [("stageA", 0), ("stageA", 1), ("stageB", 0),
                         ("stageB", 1), ("foldgate",)])
            p.op("vector", lambda e, m=ic: e.tensor_copy(
                p_b[:, m * 1024:(m + 1) * 1024], PS_C[:]),
                 reads=[("PC",)], writes=gate_w)

    def gate0(seq, t):
        return [("foldgate",)] if (seq == 0 and t < 2) else []

    def p2_A1(seq, t):
        g0 = gate0(seq, t)
        j = t % 2
        cl = new_cluster()
        for c in range(NCH):
            p.op("tensor", lambda e, cc=c, tt=t: e.matmul(
                PS_A[:, cc * 128:(cc + 1) * 128],
                wq_b[:, cc * 128:(cc + 1) * 128], xslice(cc, tt),
                start=True, stop=True, skip_group_check=True),
                 reads=[("xnc", t), ("wq",)], writes=[("PA",)], cluster=cl)
        p.op("scalar", lambda e: e.activation(relu_b[j][:], PS_A[:, 0:1024], AF.Relu),
             reads=[("PA",)], writes=[("relu", j, 0), ("relu", j, 1)])
        p.op("scalar", lambda e: e.activation(exp_b[j][:], PS_A[:, 0:1024], AF.Exp),
             reads=[("PA",)], writes=[("exp", j, 0), ("exp", j, 1)])
        p.op("vector", lambda e: e.scalar_tensor_tensor(
            qf_b[j], relu_b[j][:], 1.0, exp_b[j][:], ALU.add, ALU.min),
             reads=[("relu", j, 0), ("relu", j, 1), ("exp", j, 0), ("exp", j, 1)] + g0,
             writes=[("qf", j)])

    def p2_A2(seq, t):
        g0 = gate0(seq, t)
        j = t % 2
        cl = new_cluster()
        for c in range(NCH):
            p.op("tensor", lambda e, cc=c: e.matmul(
                PS_A[0:16, 1024:1152], zblk_b[:, cc * 16:(cc + 1) * 16],
                qf_b[j][:, cc * 128:(cc + 1) * 128],
                start=(cc == 0), stop=(cc == NCH - 1), skip_group_check=True),
                 reads=[("qf", j), ("zblk",)], writes=[("PDS",)], cluster=cl)
        p.op("vector", lambda e: e.tensor_scalar_add(
            den_f[j][:], PS_A[0:16, 1024:1152], float(EPS_ATTN)),
             reads=[("PDS",)], writes=[("den", j)])
        orc = p.op("vector", lambda e: e.reciprocal(rden_f[j][:], den_f[j][:]),
                   reads=[("den", j)], writes=[("rdenf", j)])
        p.op("vector", lambda e: e.tensor_copy(rden_b[j][:], rden_f[j][:]),
             reads=[("rdenf", j)], writes=[("rdenb", j)], force=[orc])
        for hh in range(2):
            cl = new_cluster()
            for c4 in range(4):
                cc0 = hh * 4 + c4
                p.op("tensor", lambda e, cc=cc0, c2=c4: e.matmul(
                    PS_A[:, 1024 + c2 * 128:1024 + (c2 + 1) * 128],
                    sel_b[:, cc * 128:(cc + 1) * 128], rden_b[j][:],
                    start=True, stop=True, skip_group_check=True),
                     reads=[("rdenb", j), ("sel",)], writes=[("PDS",)], cluster=cl)
            p.op("vector", lambda e, h2=hh: e.tensor_copy(
                rdenbc_b[j][:, h2 * 512:(h2 + 1) * 512], PS_A[:, 1024:1536]),
                 reads=[("PDS",)] + g0, writes=[("rdenbc", j, hh)])

        p.op("gpsimd", lambda e: e.tensor_tensor(
            qf2_b[j], qf_b[j], rdenbc_b[j], ALU.mult),
             reads=[("qf", j), ("rdenbc", j, 0), ("rdenbc", j, 1)],
             writes=[("qf2", j)])
        for hh in range(2):
            cl = new_cluster()
            for c4 in range(4):
                cc0 = hh * 4 + c4
                p.op("tensor", lambda e, cc=cc0, c2=c4: e.matmul(
                    PS_A[:, 1024 + c2 * 128:1024 + (c2 + 1) * 128],
                    kvblk_b[:, cc * 128:(cc + 1) * 128],
                    qf2_b[j][:, cc * 128:(cc + 1) * 128],
                    start=True, stop=True, skip_group_check=True),
                     reads=[("qf2", j), ("kvblk",)], writes=[("PDS",)], cluster=cl)
            p.op("scalar", lambda e, h2=hh: e.copy(
                ctx_b[j][:, h2 * 512:(h2 + 1) * 512], PS_A[:, 1024:1536]),
                 reads=[("PDS",)] + g0, writes=[("ctx", j, hh)])


    def p2_A2b(seq, t):
        j = t % 2
        cl = new_cluster()
        for hh in range(2):
            for c in range(NCH):
                p.op("tensor", lambda e, cc=c, h2=hh: e.matmul(
                    PS_C[:, h2 * 512:(h2 + 1) * 512],
                    ctx_b[j][:, cc * 128:(cc + 1) * 128],
                    wo_b[:, cc * 1024 + h2 * 512: cc * 1024 + (h2 + 1) * 512],
                    start=(cc == 0), stop=(cc == NCH - 1), skip_group_check=True),
                     reads=[("ctx", j, 0), ("ctx", j, 1)] + WOK,
                     writes=[("PC",)], cluster=cl)

    def p2_B(seq, t, xk):
        g0 = gate0(seq, t)
        j = t % 2
        xbuf, xkey = xk
        ln_stats(st2[j], ("s2", j), PS_C[:], [("PC",)])
        ln_rsqrt(st2[j], ("s2", j), EPS_LN)
        ln_mb(st2[j], ("s2", j))
        ln_apply_s(st2[j], ("s2", j), lnS[j], PS_C[:], [("PC",)] + g0, [("lnS", j)])
        p.op("vector", lambda e: e.tensor_add(x1_b[j], xbuf[:], lnS[j]),
             reads=[xkey, ("lnS", j)] + g0, writes=[("x1", j)])
        ln_stats(st3[j], ("s3", j), x1_b[j], [("x1", j)])
        ln_rsqrt(st3[j], ("s3", j), EPS_LN)
        ln_mb(st3[j], ("s3", j))
        ln_apply_v(st3[j], ("s3", j), xn2tm_b[j], x1_b[j], [("x1", j)] + g0,
                   [("xn2tm", j)])
        cl = new_cluster()
        for c in range(NCH):
            p.op("tensor", lambda e, cc=c: e.transpose(
                PS_T[:, cc * 128:(cc + 1) * 128],
                xn2tm_b[j][:, cc * 128:(cc + 1) * 128], eye_b[:]),
                 reads=[("xn2tm", j), ("eye",)], writes=[("PT",)], cluster=cl)
        p.op("vector", lambda e: e.tensor_copy(xn2f_b[j], PS_T[:]),
             reads=[("PT",)] + g0, writes=[("xn2f", j)])
        cl = new_cluster()
        for hh in range(2):
            p.op("tensor", lambda e, h2=hh: e.matmul(
                PS_B[:, h2 * 512:(h2 + 1) * 512], onesr_b[0:1, :],
                bias_b[0:1, h2 * 512:(h2 + 1) * 512],
                start=True, stop=False, skip_group_check=True),
                 reads=[("onesr",), ("biasb",)], writes=[("PB",)], cluster=cl)
        for hh in range(2):
            for c in range(NCH):
                p.op("tensor", lambda e, cc=c, h2=hh: e.matmul(
                    PS_B[:, h2 * 512:(h2 + 1) * 512],
                    xn2f_b[j][:, cc * 128:(cc + 1) * 128],
                    p_b[:, cc * 1024 + h2 * 512: cc * 1024 + (h2 + 1) * 512],
                    start=False, stop=(cc == NCH - 1), skip_group_check=True),
                     reads=[("xn2f", j), ("p_b",)], writes=[("PB",)], cluster=cl)
        p.op("scalar", lambda e: e.activation(t_b[j], PS_B[:], AF.Exp, scale=2.0),
             reads=[("PB",)] + g0, writes=[("t_b", j)])

    def p2_C(seq, t):
        g0 = gate0(seq, t)
        j = t % 2
        p.op("scalar", lambda e: e.activation(t_b[j], t_b[j], AF.Ln, bias=one_s[:]),
             reads=[("t_b", j), ("one",)], writes=[("t_b", j)])
        p.op("scalar", lambda e: e.activation(s_b[j], t_b[j], AF.Exp, scale=-1.0),
             reads=[("t_b", j)] + g0, writes=[("s_b", j)])
        ln_stats(st4[j], ("s4", j), s_b[j], [("s_b", j)])
        ln_rsqrt(st4[j], ("s4", j), EPS_LN / 4.0)
        ln_mb(st4[j], ("s4", j))
        ln_apply_v(st4[j], ("s4", j), lnS4[j], s_b[j], [("s_b", j)] + g0,
                   [("lnS4", j)])
        p.op("gpsimd", lambda e: e.tensor_tensor(
            out_b[j][:], x1_b[j], lnS4[j], ALU.subtract),
             reads=[("x1", j), ("lnS4", j)], writes=[("out", j)])
        row0 = seq * SEQLEN + t * 128
        dma("gpsimd", y_d[row0:row0 + 128, :], out_b[j][:], reads=[("out", j)])

    def run_pass2(seq):
        xk = {0: load_x(seq, 0), 1: load_x(seq, 1)}
        for slot in range(NT + 2):
            tA, tB, tC = slot, slot - 1, slot - 2
            if tA < NT:
                if tA + 2 < NT:
                    xk[tA + 2] = load_x(seq, tA + 2)
                p2_A1(seq, tA)
                p2_A2(seq, tA)
            if 0 <= tB < NT:
                p2_B(seq, tB, xk[tB])
            if tA < NT:
                p2_A2b(seq, tA)
            if 0 <= tC < NT:
                p2_C(seq, tC)

    run_pass1(0)
    kv_evac()
    fold()
    run_pass2(0)
    run_pass1(1)
    kv_evac()
    run_pass2(1)

    import os
    if os.environ.get("KSCHED", "1") == "1":
        p.schedule()
    p.resolve()
    build_nc.last_plan = p

    with ExitStack() as sst:
        sems = {name: sst.enter_context(nc.semaphore(name))
                for name in p.final_counts}
        with nc.allow_low_precision("deliberate bf16 matmul-operand casts; "
                                    "fp32 accumulation and elementwise math"):
            with nc.Block() as block:

                @block.sync
                def _(eng):
                    p.emit_engine("sync", eng, sems)
                    for name, total in sorted(p.final_counts.items()):
                        if name.startswith("dma_"):
                            eng.wait_ge(sems[name], total)

                @block.tensor
                def _(eng):
                    p.emit_engine("tensor", eng, sems)

                @block.vector
                def _(eng):
                    p.emit_engine("vector", eng, sems)

                @block.scalar
                def _(eng):
                    p.emit_engine("scalar", eng, sems)

                @block.gpsimd
                def _(eng):
                    p.emit_engine("gpsimd", eng, sems)

    return nc


def kernel(L0_attn_query, L0_attn_Wq, L0_attn_Wk, L0_attn_Wv, L0_attn_Wo,
           L0_ffn_weight, L0_ffn_aux_weight1, L0_ffn_aux_weight2, L0_ffn_bias):
    f32 = np.float32
    x = np.ascontiguousarray(np.asarray(L0_attn_query, dtype=f32))
    Wq = np.asarray(L0_attn_Wq, dtype=f32)
    Wk = np.asarray(L0_attn_Wk, dtype=f32)
    Wv = np.asarray(L0_attn_Wv, dtype=f32)

    wqblk = np.zeros((NCH, 128, 128), dtype=f32)
    wkvblk = np.zeros((NCH, 128, 256), dtype=f32)
    for c in range(NCH):
        h0, h1 = 2 * c, 2 * c + 1
        wqblk[c, 0:64, 0:64] = Wq[h0]
        wqblk[c, 64:128, 64:128] = Wq[h1]
        wkvblk[c, 0:64, 0:64] = Wk[h0]
        wkvblk[c, 64:128, 64:128] = Wk[h1]
        wkvblk[c, 0:64, 128:192] = Wv[h0]
        wkvblk[c, 64:128, 192:256] = Wv[h1]
    sel = np.zeros((16, NCH * 128), dtype=f32)
    for c in range(NCH):
        sel[2 * c, c * 128:c * 128 + 64] = 1.0
        sel[2 * c + 1, c * 128 + 64:c * 128 + 128] = 1.0
    common = {
        "wqblk": wqblk,
        "wkvblk": wkvblk,
        "wo": np.ascontiguousarray(np.asarray(L0_attn_Wo, dtype=f32)),
        "wt": np.ascontiguousarray(np.asarray(L0_ffn_weight, dtype=f32).T),
        "a1": np.ascontiguousarray(np.asarray(L0_ffn_aux_weight1, dtype=f32)),
        "a2": np.ascontiguousarray(np.asarray(L0_ffn_aux_weight2, dtype=f32)),
        "bias": np.ascontiguousarray(
            np.asarray(L0_ffn_bias, dtype=f32).reshape(1, D)),
        "sel": sel,
        "eye": np.eye(128, dtype=f32),
    }
    B = x.shape[0]
    xr = x.reshape(B, SEQLEN, D)
    in_maps = []
    for c in range(NCORES):
        xc = np.ascontiguousarray(
            xr[c * SEQ_PER_CORE:(c + 1) * SEQ_PER_CORE].reshape(ROWS, D))
        in_maps.append({"x": xc, **common})

    nc = build_nc()
    res = run_bass_kernel_spmd(nc, in_maps, core_ids=list(range(NCORES)))
    out = np.empty((B, SEQLEN, D), dtype=np.float32)
    for c in range(NCORES):
        out[c * SEQ_PER_CORE:(c + 1) * SEQ_PER_CORE] = \
            res.results[c]["y"].reshape(SEQ_PER_CORE, SEQLEN, D)
    return out.reshape(x.shape)
